# version 04
# baseline (speedup 1.0000x reference)
import contextlib
import numpy as np
import ml_dtypes
import concourse.bass as bass
import concourse.mybir as mybir
from concourse.bacc import Bacc
from concourse.bass_utils import run_bass_kernel_spmd

F32 = mybir.dt.float32
F16 = mybir.dt.float16
BF16 = mybir.dt.bfloat16
I32 = mybir.dt.int32
I16 = mybir.dt.int16
AF = mybir.ActivationFunctionType
ALU = mybir.AluOpType
AX = mybir.AxisListType
ds = bass.ds

S = 8192
D = 4096
H = 32
DN = 128
DV = 128
DQ = 192
INTER = 11008
NH = 16
HG = 4
CHTOK = 1024
SCALE = float(DQ) ** -0.5
EPS = 1e-5
MAGIC0 = 0x5A5A1234
MAGIC1 = 0x0BADF00D

NKT = S // 128
NSW = 4
NIW = 22
IW_COLS = [512] * 21 + [256]
NIC = INTER // 128
NFC = D // 128


def _masks_np():
    kj = np.arange(128)[:, None]
    qj = np.arange(512)[None, :]
    parts = [(qj >= 128 * mi + kj).astype(np.float32) for mi in range(4)]
    parts.append(np.ones((128, 512), np.float32))
    parts.append(np.zeros((128, 512), np.float32))
    return np.concatenate(parts, axis=1).astype(ml_dtypes.bfloat16)


def _moff_np(p):
    out = np.zeros((NSW, NKT), dtype=np.int32)
    for sw in range(NSW):
        ch = p if sw < 2 else 7 - p
        qbase = CHTOK * ch + 512 * (sw % 2)
        for kt in range(NKT):
            ks = kt * 128
            if ks + 128 <= qbase:
                idx = 4
            elif ks >= qbase + 512:
                idx = 5
            else:
                idx = (ks - qbase) // 128
            out[sw, kt] = idx * 512
    return np.ascontiguousarray(out.reshape(1, NSW * NKT))


def _bigsel_g_np():
    b = np.zeros((128, 56), dtype=np.float32)
    for j in range(128):
        b[j, 24 + j // 16] = 1.0
    return b.astype(ml_dtypes.bfloat16)


def _bigsel_d_np():
    b = np.zeros((128, 172), dtype=np.float32)
    b[:, 86] = 1.0
    return b.astype(ml_dtypes.bfloat16)


class Ctr:
    def __init__(self):
        self.v = {}

    def inc(self, name, n):
        self.v[name] = self.v.get(name, 0) + n
        return self.v[name]

    def val(self, name):
        return self.v.get(name, 0)


PH_ORDER = ["consts", "zp", "kv", "attn", "bar", "wo", "ln1", "ffnt", "gateup", "down", "ln2"]


def build_program(dbg=(), stop_after=None):
    lim = len(PH_ORDER) if stop_after is None else PH_ORDER.index(stop_after)

    def PH(name):
        return PH_ORDER.index(name) <= lim

    nc = Bacc("TRN2", target_bir_lowering=False)

    def din(name, shape, dt):
        return nc.dram_tensor(name, shape, dt, kind="ExternalInput").ap()

    xT = din("xT", [D, S], F32)
    xqT = din("xqT", [D, 2 * CHTOK], F32)
    xres = din("xres", [CHTOK, D], F32)
    wq_s = din("wq_s", [D, NH * DN], F32)
    wk_s = din("wk_s", [D, NH * DN], F32)
    wv_s = din("wv_s", [D, NH * DV], F32)
    wo_f = din("wo_f", [H * DV, D], F32)
    g_q = din("g_q", [D // 8, INTER], I32)
    g_s = din("g_s", [32, INTER], F16)
    g_z = din("g_z", [32, INTER // 8], I32)
    u_q = din("u_q", [D // 8, INTER], I32)
    u_s = din("u_s", [32, INTER], F16)
    u_z = din("u_z", [32, INTER // 8], I32)
    d_q = din("d_q", [INTER // 8, D], I32)
    d_s = din("d_s", [NIC, D], F16)
    d_z = din("d_z", [NIC, D // 8], I32)
    masksd = din("masksd", [128, 6 * 512], BF16)
    moffd = din("moffd", [1, NSW * NKT], I32)
    bigg = din("bigg", [128, 56], BF16)
    bigd = din("bigd", [128, 172], BF16)
    cfg = din("cfg", [1, 8], I32)

    y = nc.dram_tensor("y", [CHTOK, D], F32, kind="ExternalOutput").ap()
    tmg = nc.dram_tensor("tmg", [1, 4], I32, kind="ExternalOutput").ap()
    dbg_t = {}
    if dbg:
        shp = {
            "kv": ([128, S + 32 * 512], BF16),
            "q": ([128, 2 * CHTOK], BF16),
            "attn": ([H * DV, 2 * CHTOK], BF16),
            "wo": ([CHTOK, D], F32),
            "x1": ([CHTOK, D], F32),
            "inter": ([256, CHTOK], BF16),
            "d16": ([CHTOK, D], F16),
        }
        for k in dbg:
            sh, dt = shp[k]
            dbg_t[k] = nc.dram_tensor("dbg_" + k, sh, dt, kind="ExternalOutput").ap()

    kT16 = nc.dram_tensor("kT16", [NH * 128, S], BF16).ap()
    V16 = nc.dram_tensor("V16", [HG * 128, (S // 128) * 512], BF16).ap()
    qT16 = nc.dram_tensor("qT16", [NH * 128, 2 * CHTOK], BF16).ap()
    wo_out = nc.dram_tensor("wo_out", [CHTOK, D], F32).ap()
    x1sp = nc.dram_tensor("x1sp", [CHTOK, D], F32).ap()
    interH = nc.dram_tensor("interH", [NIC * 128, CHTOK], BF16).ap()
    d16H = nc.dram_tensor("d16H", [CHTOK, D], F16).ap()
    zpgH = nc.dram_tensor("zpgH", [32, INTER], BF16).ap()
    zpuH = nc.dram_tensor("zpuH", [32, INTER], BF16).ap()
    zpdH = nc.dram_tensor("zpdH", [NIC, D], BF16).ap()

    attnT_sh = nc.dram_tensor("attnT_sh", [H * DV, 2 * CHTOK], BF16, addr_space="Shared").ap()
    flags = nc.dram_tensor("flags", [2, 16], I32, addr_space="Shared").ap()

    C = Ctr()

    with contextlib.ExitStack() as stk:
        def sb(name, shape, dt):
            return stk.enter_context(nc.sbuf_tensor(name, shape, dt))

        masks = sb("masks", [128, 6 * 512], BF16)
        moffs = sb("moffs", [1, NSW * NKT], I32)
        cfgs = sb("cfgs", [1, 8], I32)
        ones128 = sb("ones128", [128, 128], BF16)
        biggs = sb("biggs", [128, 56], BF16)
        bigds = sb("bigds", [128, 172], BF16)
        shv = sb("shv", [128, 1], I32)
        sh16 = [sb(f"sh16_{r}", [128, 1], I16) for r in range(4)]
        mk16 = sb("mk16", [128, 1], I16)
        epsv = sb("epsv", [128, 1], F32)
        fcell = sb("fcell", [1, 4], I32)
        fpoll = sb("fpoll", [1, 4], I32)

        slab0 = sb("slab0", [128, 16384], BF16)
        slab1 = sb("slab1", [128, 16384], BF16)
        lnA = slab0[:].bitcast(F32)[:, 0:4096]
        lnB = slab0[:].bitcast(F32)[:, 4096:8192]
        dump = slab1[:].bitcast(F32)[:, 0:4096]
        x1w16 = slab1[:, 8192:12288]
        arena = sb("arena", [128, 49152], BF16)
        evst = sb("evst", [128, 4 * 512], BF16)
        evf = sb("evf", [128, 1024], F32)
        pbuf = sb("pbuf", [128, 2 * 512], BF16)
        pmb = sb("pmb", [128, 2 * 512], BF16)
        recb = sb("recb", [128, 512], F32)
        stat = sb("stat", [128, 8], F32)

        s2b = sb("s2b", [128, 4 * 1024], BF16)
        zob = sb("zob", [128, 4 * 1024], BF16)
        zodl2 = sb("zodl2", [128, 512], BF16)
        silu_sb = sb("silu_sb", [128, 4 * CHTOK], BF16)
        ichunk = slab1[:, 14848:15872]
        srep = slab1[:, 13824:14336]
        wd16 = slab1[:, 13312:13824]
        q4d = slab1[:, 12288:13312].bitcast(I32)
        q4g = slab1[:, 12288:13312].bitcast(I16)
        d16b = slab1[:, 14336:14848].bitcast(F16)

        ps1 = stk.enter_context(nc.psum_tensor("ps1", [128, 2048], F32))
        ps2 = stk.enter_context(nc.psum_tensor("ps2", [128, 2048], F32))

        sem_names = ["qL", "qH", "qS", "vC", "aC", "pM", "oS", "bR", "bI"]
        sems = {n: stk.enter_context(nc.semaphore(n)) for n in sem_names}
        block = stk.enter_context(nc.Block())
        regs = {}

        def W(eng, name, val=None):
            eng.wait_ge(sems[name], C.val(name) if val is None else val)

        def I(ins, name, n=1):
            ins.then_inc(sems[name], n)
            return C.inc(name, n)

        @block.sync
        def _(sync):
            I(sync.dma_start(out=masks[:], in_=masksd[:, :]), "qH", 16)
            I(sync.dma_start(out=moffs[:], in_=moffd[:, :]), "qH", 16)
            I(sync.dma_start(out=cfgs[:], in_=cfg[:, :]), "qH", 16)
            I(sync.dma_start(out=biggs[:], in_=bigg[:, :]), "qH", 16)
            I(sync.dma_start(out=bigds[:], in_=bigd[:, :]), "qH", 16)

        @block.gpsimd
        def _(g):
            g.iota(shv[:], pattern=[[0, 1]], base=0, channel_multiplier=4)
            I(g.memset(fcell[:], 0), "bR")

        @block.vector
        def _(v):
            v.memset(ones128[:], 1.0)
            for r in range(4):
                v.memset(sh16[r][:], 4 * r)
            v.memset(mk16[:], 15)
            v.memset(epsv[:], EPS)
            W(v, "bR")
            I(v.tensor_scalar(out=shv[:], in0=shv[:], scalar1=31, scalar2=None,
                              op0=ALU.bitwise_and), "vC")

        def emit_zp(src_z, dstH, rows, zcols):
            CC = 2752 if zcols == INTER else 1024
            for cc in range(zcols // CC):
                @block.sync
                def _(sync):
                    W(sync, "qS")
                    W(sync, "vC")
                    I(sync.dma_start(out=lnA[:].bitcast(I32)[0:rows, 0:CC // 8],
                                     in_=src_z[:, cc * CC // 8:(cc + 1) * CC // 8]), "qH", 16)

                @block.vector
                def _(v):
                    W(v, "qH")
                    zi = dump[:].bitcast(I32)
                    for r in range(8):
                        v.tensor_scalar(
                            out=zi[0:rows, r:CC:8],
                            in0=lnA[:].bitcast(I32)[0:rows, 0:CC // 8],
                            scalar1=4 * r, scalar2=15,
                            op0=ALU.logical_shift_right, op1=ALU.bitwise_and)
                    I(v.tensor_scalar(out=shv[0:1, 0:1], in0=shv[0:1, 0:1], scalar1=0,
                                      scalar2=None, op0=ALU.bitwise_or), "vC")
                    W(v, "vC")
                    I(v.tensor_scalar(
                        out=slab1[0:rows, 8192:8192 + CC],
                        in0=zi[0:rows, 0:CC], scalar1=1.0, scalar2=None,
                        op0=ALU.add), "vC")

                @block.scalar
                def _(sc):
                    W(sc, "vC")
                    I(sc.dma_start(out=dstH[0:rows, cc * CC:(cc + 1) * CC],
                                   in_=slab1[0:rows, 8192:8192 + CC]), "qS", 16)

        if PH("zp"):
            emit_zp(g_z, zpgH, 32, INTER)
            emit_zp(u_z, zpuH, 32, INTER)
            emit_zp(d_z, zpdH, NIC, D)

        wq16 = arena[:, 0:16384]
        wk16 = arena[:, 16384:32768]
        wv16 = arena[:, 32768:49152]
        slabs = [slab0, slab1]
        vC_hist = {}

        for hg in range(HG if PH("kv") else 0):
            @block.gpsimd
            def _(g):
                W(g, "qS")
                W(g, "pM")
                for fc in range(NFC):
                    I(g.dma_start(out=wq16[:, fc * 512:fc * 512 + 512],
                                  in_=wq_s[fc * 128:fc * 128 + 128, hg * 512:hg * 512 + 512]), "qL", 16)
                    I(g.dma_start(out=wk16[:, fc * 512:fc * 512 + 512],
                                  in_=wk_s[fc * 128:fc * 128 + 128, hg * 512:hg * 512 + 512]), "qL", 16)
                    I(g.dma_start(out=wv16[:, fc * 512:fc * 512 + 512],
                                  in_=wv_s[fc * 128:fc * 128 + 128, hg * 512:hg * 512 + 512]), "qL", 16)
            w16_qL = C.val("qL")

            def kv_window(win, wsrc, c0, is_q, hg=hg, w16_qL=w16_qL):
                par = win % 2
                sl = slabs[par]

                @block.gpsimd
                def _(g):
                    if win >= 2:
                        W(g, "vC", vC_hist[(hg, win - 2)])
                    for fc in range(NFC):
                        I(g.dma_start(out=sl[:, fc * 512:fc * 512 + 512],
                                      in_=wsrc[fc * 128:fc * 128 + 128, c0:c0 + 512]), "qL", 16)

                @block.tensor
                def _(t):
                    W(t, "qL")
                    if win == 0:
                        W(t, "qL", w16_qL)
                    if win >= 1:
                        W(t, "vC", vC_hist[(hg, win - 1)])
                    for fc in range(NFC):
                        st = fc == 0
                        sp = fc == NFC - 1
                        wsel = wq16 if is_q else wk16
                        for hc in range(4):
                            ins = t.matmul(ps1[:, hc * 512:hc * 512 + 512],
                                           wsel[:, fc * 512 + hc * 128:fc * 512 + hc * 128 + 128],
                                           sl[:, fc * 512:fc * 512 + 512],
                                           start=st, stop=sp, skip_group_check=True)
                            if sp and hc == 3:
                                I(ins, "pM")
                        if not is_q:
                            for tc in range(4):
                                ins = t.matmul(ps2[:, tc * 512:tc * 512 + 512],
                                               sl[:, fc * 512 + tc * 128:fc * 512 + tc * 128 + 128],
                                               wv16[:, fc * 512:fc * 512 + 512],
                                               start=st, stop=sp, skip_group_check=True)
                                if sp and tc == 3:
                                    I(ins, "pM")

                @block.vector
                def _(v):
                    W(v, "pM")
                    W(v, "qH")
                    for j in range(4):
                        v.tensor_copy(out=evst[:, j * 512:j * 512 + 512],
                                      in_=ps1[:, j * 512:j * 512 + 512])
                    if not is_q:
                        for j in range(4):
                            v.tensor_copy(out=evf[:].bitcast(BF16)[:, j * 512:j * 512 + 512],
                                          in_=ps2[:, j * 512:j * 512 + 512])
                    I(v.tensor_copy(out=recb[0:1, 0:1], in_=recb[0:1, 0:1]), "vC")
                vC_hist[(hg, win)] = C.val("vC")

                @block.sync
                def _(sync):
                    W(sync, "vC")
                    if not is_q:
                        for hc in range(4):
                            I(sync.dma_start(
                                out=kT16[(hg * 4 + hc) * 128:(hg * 4 + hc) * 128 + 128,
                                         win * 512:win * 512 + 512],
                                in_=evst[:, hc * 512:hc * 512 + 512]), "qH", 16)
                        for tc in range(4):
                            I(sync.dma_start(
                                out=V16[hg * 128:hg * 128 + 128,
                                        (4 * win + tc) * 512:(4 * win + tc) * 512 + 512],
                                in_=evf[:].bitcast(BF16)[:, tc * 512:tc * 512 + 512]), "qH", 16)
                    else:
                        for hc in range(4):
                            I(sync.dma_start(
                                out=qT16[(hg * 4 + hc) * 128:(hg * 4 + hc) * 128 + 128,
                                         (win - 16) * 512:(win - 16) * 512 + 512],
                                in_=evst[:, hc * 512:hc * 512 + 512]), "qH", 16)

            for win in range(16):
                kv_window(win, xT, win * 512, False)
            for win in range(16, 20):
                kv_window(win, xqT, (win - 16) * 512, True)

        if "kv" in dbg_t:
            @block.gpsimd
            def _(g):
                W(g, "qH")
                I(g.dma_start(out=dbg_t["kv"][:, 0:S], in_=kT16[0:128, :]), "oS", 16)
                I(g.dma_start(out=dbg_t["kv"][:, S:S + 16384], in_=V16[0:128, 0:16384]), "oS", 16)

        if "q" in dbg_t:
            @block.gpsimd
            def _(g):
                W(g, "qH")
                I(g.dma_start(out=dbg_t["q"][:, :], in_=qT16[0:128, :]), "oS", 16)

        ktsb = arena[:, 0:32768]
        qtsb = arena[:, 32768:40960]
        vsb = [slab0, slab1]

        for hg in range(HG if PH("attn") else 0):
            @block.sync
            def _(sync):
                W(sync, "oS")
                W(sync, "qH")
                W(sync, "pM")
                for hi in range(4):
                    I(sync.dma_start(out=ktsb[:, hi * 8192:(hi + 1) * 8192],
                                     in_=kT16[(hg * 4 + hi) * 128:(hg * 4 + hi) * 128 + 128, :]), "qH", 16)
                    I(sync.dma_start(out=qtsb[:, hi * 2048:(hi + 1) * 2048],
                                     in_=qT16[(hg * 4 + hi) * 128:(hg * 4 + hi) * 128 + 128, :]), "qH", 16)
                I(sync.dma_start(out=slab0[:], in_=V16[hg * 128:hg * 128 + 128, 0:16384]), "qH", 16)
                I(sync.dma_start(out=slab1[:], in_=V16[hg * 128:hg * 128 + 128, 16384:32768]), "qH", 16)
            ld_qH = C.val("qH")

            for hi in range(4):
                for sw in range(NSW):
                    sVal = {}
                    expVal = {}
                    maskVal = {}
                    pvVal = {}

                    def emit_pv(t, kt, hi=hi):
                        t.wait_ge(sems["vC"], maskVal[kt])
                        t.matmul(ps2[:, 0:512],
                                 vsb[kt // 32][:, (kt % 32) * 512 + hi * 128:
                                               (kt % 32) * 512 + hi * 128 + 128],
                                 pmb[:, (kt % 2) * 512:(kt % 2) * 512 + 512],
                                 start=(kt == 0), stop=(kt == NKT - 1),
                                 skip_group_check=True)
                        ins = t.matmul(ps2[:, 512:1024], ones128[:],
                                       pmb[:, (kt % 2) * 512:(kt % 2) * 512 + 512],
                                       start=(kt == 0), stop=(kt == NKT - 1),
                                       skip_group_check=True)
                        ins.then_inc(sems["pM"], 2)
                        C.inc("pM", 2)
                        pvVal[kt] = C.val("pM")

                    for kt in range(NKT):
                        slot = kt % 4
                        spsum = ps1[:, slot * 512:slot * 512 + 512]

                        @block.tensor
                        def _(t):
                            if kt == 0 and hi == 0 and sw == 0:
                                W(t, "qH", ld_qH)
                            if kt >= 4:
                                t.wait_ge(sems["aC"], expVal[kt - 4])
                            ins = t.matmul(
                                spsum,
                                ktsb[:, hi * 8192 + kt * 128:hi * 8192 + kt * 128 + 128],
                                qtsb[:, hi * 2048 + sw * 512:hi * 2048 + sw * 512 + 512],
                                start=True, stop=True, skip_group_check=True)
                            sVal[kt] = I(ins, "pM")
                            if kt >= 2:
                                emit_pv(t, kt - 2)

                        @block.scalar
                        def _(sc):
                            sc.wait_ge(sems["pM"], sVal[kt])
                            if kt >= 2:
                                sc.wait_ge(sems["vC"], maskVal[kt - 2])
                            ins = sc.activation(out=pbuf[:, (kt % 2) * 512:(kt % 2) * 512 + 512],
                                                in_=spsum, func=AF.Exp, scale=SCALE)
                            expVal[kt] = I(ins, "aC")

                        @block.vector
                        def _(v):
                            v.wait_ge(sems["aC"], expVal[kt])
                            if kt >= 2:
                                v.wait_ge(sems["pM"], pvVal[kt - 2])
                            if "mr" not in regs:
                                regs["mr"] = v.alloc_register("mr")
                            v.load([regs["mr"]], moffs[0:1, sw * NKT + kt:sw * NKT + kt + 1])
                            mo = bass.RuntimeValue(regs["mr"], min_val=0, max_val=5 * 512)
                            ins = v.tensor_tensor(out=pmb[:, (kt % 2) * 512:(kt % 2) * 512 + 512],
                                                  in0=pbuf[:, (kt % 2) * 512:(kt % 2) * 512 + 512],
                                                  in1=masks[:, ds(mo, 512)], op=ALU.mult)
                            maskVal[kt] = I(ins, "vC")

                    @block.tensor
                    def _(t):
                        emit_pv(t, NKT - 2)
                        emit_pv(t, NKT - 1)

                    @block.vector
                    def _(v):
                        W(v, "pM")
                        W(v, "oS")
                        I(v.reciprocal(out=recb[:], in_=ps2[:, 512:1024]), "vC")
                        W(v, "vC")
                        I(v.tensor_tensor(out=evst[:, 0:512], in0=ps2[:, 0:512],
                                          in1=recb[:], op=ALU.mult), "vC")

                    @block.gpsimd
                    def _(g):
                        W(g, "vC")
                        if "rr" not in regs:
                            regs["rr"] = g.alloc_register("rowbase")
                            g.load([regs["rr"]], cfgs[0:1, 1:2])
                        rv = bass.RuntimeValue(regs["rr"], min_val=0, max_val=2048)
                        I(g.dma_start(
                            out=attnT_sh[ds(rv + (hg * 4 + hi) * 128, 128),
                                         sw * 512:sw * 512 + 512],
                            in_=evst[:, 0:512]), "oS", 16)

                    @block.tensor
                    def _(t):
                        W(t, "vC")

        @block.gpsimd
        def _(g):
            if not PH("bar"):
                return
            W(g, "oS")
            g.memset(fcell[0:1, 0:1], MAGIC0)
            g.memset(fcell[0:1, 1:2], MAGIC1)
            I(g.memset(fcell[0:1, 2:3], 0), "bR")
            W(g, "bR")
            regs["hr"] = g.alloc_register("hreg")
            g.load([regs["hr"]], cfgs[0:1, 0:1])
            hv = bass.RuntimeValue(regs["hr"], min_val=0, max_val=1)
            I(g.dma_start(out=flags[ds(hv, 1), 0:4], in_=fcell[:, :]), "oS", 16)
            W(g, "oS")
            cnt = g.alloc_register("cnt")
            g.reg_mov(cnt, C.val("bR"))
            r0 = g.alloc_register("m0")
            r1 = g.alloc_register("m1")
            okr = g.alloc_register("ok")
            g.reg_mov(okr, 0)

            def cond():
                return bass.RuntimeValue(okr, min_val=0, max_val=1) == 0

            with g.While(cond):
                g.reg_add(cnt, cnt, 16)
                g.dma_start(out=fpoll[:, :], in_=flags[ds(1 - hv, 1), 0:4]).then_inc(sems["bR"], 16)
                g.wait_ge(sems["bR"], bass.RuntimeValue(cnt, min_val=0))
                g.load([r0, r1], fpoll[0:1, 0:2])
                a = bass.RuntimeValue(r0) == MAGIC0
                b = bass.RuntimeValue(r1) == MAGIC1
                g.reg_alu(okr, a, b, op=ALU.logical_and)
            g.reg_save(fcell[0:1, 3:4], bass.RuntimeValue(cnt, min_val=0))
            I(g.dma_start(out=tmg[:, :], in_=fcell[:, :]), "oS", 16)
            regs["cr"] = g.alloc_register("creg")
            g.load([regs["cr"]], cfgs[0:1, 2:3])
            cv = bass.RuntimeValue(regs["cr"], min_val=0, max_val=1024)
            for dc in range(32):
                I(g.dma_start(out=arena[:, dc * 1024:dc * 1024 + 1024],
                              in_=attnT_sh[dc * 128:dc * 128 + 128, ds(cv, 1024)]), "qL", 16)

        if "attn" in dbg_t:
            @block.gpsimd
            def _(g):
                I(g.dma_start(out=dbg_t["attn"][:, :], in_=attnT_sh[:, :]), "oS", 16)

        attw = arena
        wo_buf = [slab0, slab1]
        for dw in range(8 if PH("wo") else 0):
            par = dw % 2

            @block.gpsimd
            def _(g):
                if dw >= 2:
                    W(g, "pM")
                for dc in range(32):
                    I(g.dma_start(out=wo_buf[par][:, dc * 512:dc * 512 + 512],
                                  in_=wo_f[dc * 128:dc * 128 + 128, dw * 512:dw * 512 + 512]), "qL", 16)

            for tc in range(8):
                ppar = tc % 2

                @block.tensor
                def _(t):
                    W(t, "qL")
                    W(t, "vC")
                    for dc in range(32):
                        ins = t.matmul(ps1[:, ppar * 512:ppar * 512 + 512],
                                       attw[:, dc * 1024 + tc * 128:dc * 1024 + tc * 128 + 128],
                                       wo_buf[par][:, dc * 512:dc * 512 + 512],
                                       start=(dc == 0), stop=(dc == 31), skip_group_check=True)
                    I(ins, "pM")

                @block.vector
                def _(v):
                    W(v, "pM")
                    W(v, "qS")
                    I(v.tensor_copy(out=evf[:, ppar * 512:ppar * 512 + 512],
                                    in_=ps1[:, ppar * 512:ppar * 512 + 512]), "vC")

                @block.scalar
                def _(sc):
                    W(sc, "vC")
                    I(sc.dma_start(out=wo_out[tc * 128:tc * 128 + 128, dw * 512:dw * 512 + 512],
                                   in_=evf[:, ppar * 512:ppar * 512 + 512]), "qS", 16)

        if "wo" in dbg_t:
            @block.gpsimd
            def _(g):
                W(g, "qS")
                I(g.dma_start(out=dbg_t["wo"][:, :], in_=wo_out[:, :]), "oS", 16)

        def emit_ln(srcA, srcB, dstH, tc, b_is_f16=False):
            @block.sync
            def _(sync):
                W(sync, "qH")
                W(sync, "aC")
                W(sync, "qS")
                W(sync, "qL")
                W(sync, "pM")
                I(sync.dma_start(out=lnA[:], in_=srcA[tc * 128:tc * 128 + 128, :]), "qH", 16)
                if b_is_f16:
                    I(sync.dma_start(out=lnB[:].bitcast(F16)[:, 0:4096],
                                     in_=srcB[tc * 128:tc * 128 + 128, :]), "qH", 16)
                else:
                    I(sync.dma_start(out=lnB[:], in_=srcB[tc * 128:tc * 128 + 128, :]), "qH", 16)

            @block.vector
            def _(v):
                W(v, "qH")
                bop = lnB[:].bitcast(F16)[:, 0:4096] if b_is_f16 else lnB[:]
                I(v.tensor_tensor(out=lnA[:], in0=lnA[:], in1=bop, op=ALU.add), "vC")
                W(v, "vC")
                I(v.reduce_sum(stat[:, 0:1], lnA[:], axis=AX.X), "vC")

            @block.scalar
            def _(sc):
                W(sc, "vC")
                I(sc.activation(out=dump[:], in_=lnA[:], func=AF.Square,
                                accum_out=stat[:, 1:2]), "aC")

            @block.vector
            def _(v):
                W(v, "aC")
                v.tensor_scalar(out=stat[:, 2:3], in0=stat[:, 0:1], scalar1=1.0 / D,
                                scalar2=None, op0=ALU.mult)
                I(v.tensor_scalar(out=stat[:, 3:4], in0=stat[:, 1:2], scalar1=1.0 / D,
                                  scalar2=None, op0=ALU.mult), "vC")
                W(v, "vC")
                I(v.tensor_tensor(out=stat[:, 4:5], in0=stat[:, 2:3], in1=stat[:, 2:3],
                                  op=ALU.mult), "vC")
                W(v, "vC")
                I(v.tensor_tensor(out=stat[:, 5:6], in0=stat[:, 3:4], in1=stat[:, 4:5],
                                  op=ALU.subtract), "vC")

            @block.scalar
            def _(sc):
                W(sc, "vC")
                I(sc.activation(out=stat[:, 6:7], in_=stat[:, 5:6], func=AF.Sqrt,
                                bias=epsv[:]), "aC")

            @block.vector
            def _(v):
                W(v, "aC")
                I(v.reciprocal(out=stat[:, 0:1], in_=stat[:, 6:7]), "vC")
                W(v, "vC")
                I(v.tensor_tensor(out=stat[:, 7:8], in0=stat[:, 2:3], in1=stat[:, 0:1],
                                  op=ALU.mult), "vC")
                W(v, "vC")
                I(v.tensor_scalar(out=stat[:, 7:8], in0=stat[:, 7:8], scalar1=-1.0,
                                  scalar2=None, op0=ALU.mult), "vC")

            @block.scalar
            def _(sc):
                W(sc, "vC")
                I(sc.activation(out=dump[:], in_=lnA[:], func=AF.Identity,
                                scale=stat[:, 0:1], bias=stat[:, 7:8]), "aC")

            @block.sync
            def _(sync):
                W(sync, "aC")
                I(sync.dma_start(out=dstH[tc * 128:tc * 128 + 128, :], in_=dump[:]), "qH", 16)

        for tc in range(8 if PH("ln1") else 0):
            emit_ln(wo_out, xres, x1sp, tc)

        if "x1" in dbg_t:
            @block.gpsimd
            def _(g):
                W(g, "qH")
                I(g.dma_start(out=dbg_t["x1"][:, :], in_=x1sp[:, :]), "oS", 16)

        x1T = arena
        idsb = recb[:, 0:128]

        @block.gpsimd
        def _(g):
            if not PH("ffnt"):
                return
            W(g, "qH")
            W(g, "pM")
            g.memset(idsb[:], 0.0)
            I(g.affine_select(out=idsb[:], in_=idsb[:], compare_op=ALU.not_equal,
                              fill=1.0, base=0, pattern=[[-1, 128]],
                              channel_multiplier=1), "bI")
        idsb_bR = C.val("bI")

        for tc in range(8 if PH("ffnt") else 0):
            @block.sync
            def _(sync):
                W(sync, "pM")
                if tc == 0:
                    W(sync, "qH")
                    W(sync, "aC")
                I(sync.dma_start(out=lnA[:], in_=x1sp[tc * 128:tc * 128 + 128, :]), "qH", 16)

            for ci in range(32):
                rho, jc = ci // 4, ci % 4

                @block.tensor
                def _(t):
                    if ci == 0:
                        W(t, "qH")
                        if tc == 0:
                            W(t, "bI", idsb_bR)
                            W(t, "vC")
                    if ci >= 2 or tc > 0:
                        W(t, "vC")
                    I(t.transpose(ps2[:, (ci % 2) * 512:(ci % 2) * 512 + 128],
                                  lnA[:, 1024 * jc + rho:1024 * jc + rho + 8 * 127 + 1:8],
                                  idsb[:]), "pM")

                @block.vector
                def _(v):
                    W(v, "pM")
                    I(v.tensor_copy(out=x1T[:, ci * 1024 + tc * 128:ci * 1024 + tc * 128 + 128],
                                    in_=ps2[:, (ci % 2) * 512:(ci % 2) * 512 + 128]), "vC")

        def emit_mat(mat, iw):
            iwc = IW_COLS[iw]
            iw0 = iw * 512
            qsrc = g_q if mat == "g" else u_q
            ssrc = g_s if mat == "g" else u_s
            zH = zpgH if mat == "g" else zpuH
            is_gate = mat == "g"

            @block.sync
            def _(sync):
                W(sync, "pM")
                for jc in range(4):
                    I(sync.dma_start(
                        out=slab1[:, jc * 2048:jc * 2048 + 2 * iwc].bitcast(I32)[:, 0:iwc],
                        in_=qsrc[jc * 128:jc * 128 + 128, iw0:iw0 + iwc]), "qH", 16)


            @block.gpsimd
            def _(g):
                W(g, "pM")
                for jc in range(4):
                    src = ssrc[8 * jc:8 * jc + 8, iw0:iw0 + iwc]
                    src = src.unsqueeze(1).broadcast_to([8, 16, iwc])
                    I(g.dma_start(out=s2b[:, jc * 1024:jc * 1024 + iwc], in_=src), "qL", 16)
                    zsrc = zH[8 * jc:8 * jc + 8, iw0:iw0 + iwc]
                    zsrc = zsrc.unsqueeze(1).broadcast_to([8, 16, iwc])
                    I(g.dma_start(out=zob[:, jc * 1024:jc * 1024 + iwc], in_=zsrc), "qL", 16)

            @block.vector
            def _(v):
                W(v, "qH")
                W(v, "qL")
                W(v, "pM")
                for jc in range(4):
                    qv16 = slab1[:, jc * 2048:jc * 2048 + 2 * iwc].bitcast(I16)
                    s2v = s2b[:, jc * 1024:jc * 1024 + iwc].unsqueeze(2)
                    s2v = s2v.broadcast_to([128, iwc, 2])
                    zov = zob[:, jc * 1024:jc * 1024 + iwc].unsqueeze(2)
                    zov = zov.broadcast_to([128, iwc, 2])
                    for rp in range(4):
                        v.tensor_scalar(out=q4g[:, 0:2 * iwc], in0=qv16[:, 0:2 * iwc],
                                        scalar1=sh16[rp][:], scalar2=mk16[:],
                                        op0=ALU.logical_shift_right, op1=ALU.bitwise_and)
                        I(v.tensor_copy(out=stat[0:1, 0:1], in_=stat[0:1, 0:1]), "vC")
                        W(v, "vC")
                        v.tensor_tensor(out=pmb[:, 0:2 * iwc].bitcast(BF16) if False
                                        else pmb[:, 0:iwc * 2], in0=q4g[:, 0:2 * iwc],
                                        in1=zov, op=ALU.subtract)
                        I(v.tensor_copy(out=stat[0:1, 1:2], in_=stat[0:1, 1:2]), "vC")
                        W(v, "vC")
                        I(v.tensor_tensor(
                            out=slab0[:, (jc * 4 + rp) * 1024:(jc * 4 + rp) * 1024 + 2 * iwc],
                            in0=pmb[:, 0:2 * iwc],
                            in1=s2v, op=ALU.mult), "vC")
                        W(v, "vC")
            unpack_vC = C.val("vC")

            nnc = iwc // 128
            for ncc in range(nnc):
                for win in range(4):
                    ppar = (win * nnc + ncc) % 2

                    @block.tensor
                    def _(t):
                        W(t, "vC", unpack_vC)
                        W(t, "aC")
                        first = True
                        nmm = 0
                        for jc in range(4):
                            for rp in range(4):
                                for hp in range(2):
                                    ci = (4 * hp + rp) * 4 + jc
                                    base = (jc * 4 + rp) * 1024 + 2 * (ncc * 128) + hp
                                    wt = slab0[:, base:base + 2 * 127 + 1:2]
                                    nmm += 1
                                    ins = t.matmul(
                                        ps1[:, ppar * 512:ppar * 512 + 256],
                                        wt,
                                        x1T[:, ci * 1024 + win * 256:ci * 1024 + win * 256 + 256],
                                        start=first, stop=(nmm == 32), skip_group_check=True)
                                    first = False
                        I(ins, "pM")

                    if is_gate:
                        @block.scalar
                        def _(sc):
                            W(sc, "pM")
                            I(sc.activation(
                                out=silu_sb[:, ncc * 1024 + win * 256:ncc * 1024 + win * 256 + 256],
                                in_=ps1[:, ppar * 512:ppar * 512 + 256],
                                func=AF.Silu), "aC")
                    else:
                        @block.vector
                        def _(v):
                            W(v, "pM")
                            I(v.tensor_tensor(
                                out=ichunk[:, win * 256:win * 256 + 256],
                                in0=ps1[:, ppar * 512:ppar * 512 + 256],
                                in1=silu_sb[:, ncc * 1024 + win * 256:ncc * 1024 + win * 256 + 256],
                                op=ALU.mult), "vC")

                        @block.scalar
                        def _(sc):
                            W(sc, "vC")
                            I(sc.copy(out=stat[0:2, 2:3], in_=stat[0:2, 2:3]), "aC")

                if not is_gate:
                    ic = iw * 4 + ncc

                    @block.tensor
                    def _(t):
                        W(t, "vC")
                        for half in range(2):
                            ins = t.matmul(ps2[0:NIC, 1024 + half * 512:1536 + half * 512],
                                           bigds[:, 86 - ic:172 - ic],
                                           ichunk[:, half * 512:half * 512 + 512],
                                           start=(ic == 0), stop=(ic == NIC - 1),
                                           skip_group_check=True)
                        I(ins, "pM")

                    @block.scalar
                    def _(sc):
                        W(sc, "qS")
                        W(sc, "vC")
                        I(sc.dma_start(out=interH[ic * 128:ic * 128 + 128, :],
                                       in_=ichunk[:, 0:1024]), "qS", 16)

        for iw in range(NIW if PH("gateup") else 0):
            emit_mat("g", iw)
            emit_mat("u", iw)

        if "inter" in dbg_t:
            @block.gpsimd
            def _(g):
                W(g, "qS")
                W(g, "vC")
                I(g.dma_start(out=dbg_t["inter"][:, :], in_=interH[0:256, :]), "oS", 16)

        def dn_ps(wt):
            return (ps1[:, (wt % 4) * 512:(wt % 4) * 512 + 512] if wt < 4
                    else ps2[:, (wt - 4) * 512:(wt - 4) * 512 + 512])

        for dmg in range(8 if PH("down") else 0):
            @block.sync
            def _(sync):
                W(sync, "aC")
                if dmg == 0:
                    W(sync, "qS")
                    W(sync, "vC")
                pass

            for ic in range(NIC):
                par = ic % 2

                @block.sync
                def _(sync):
                    W(sync, "pM")
                    I(sync.dma_start(out=slab1[:, par * 1024:par * 1024 + 1024],
                                     in_=interH[ic * 128:ic * 128 + 128, :]), "qH", 16)

                @block.gpsimd
                def _(g):
                    W(g, "pM")
                    src = d_q[ic * 16:ic * 16 + 16, dmg * 512:dmg * 512 + 512]
                    src = src.unsqueeze(1).broadcast_to([16, 8, 512])
                    I(g.dma_start(out=q4d[:], in_=src), "qL", 16)
                    srow = d_s[ic:ic + 1, dmg * 512:dmg * 512 + 512].broadcast_to([128, 512])
                    I(g.dma_start(out=srep[:], in_=srow), "qL", 16)
                    zrow = zpdH[ic:ic + 1, dmg * 512:dmg * 512 + 512].broadcast_to([128, 512])
                    I(g.dma_start(out=zodl2[:], in_=zrow), "qL", 16)

                @block.vector
                def _(v):
                    W(v, "qL")
                    W(v, "pM")
                    v.tensor_scalar(out=q4d[:], in0=q4d[:],
                                    scalar1=shv[:], scalar2=15,
                                    op0=ALU.logical_shift_right, op1=ALU.bitwise_and)
                    I(v.tensor_copy(out=stat[0:1, 0:1], in_=stat[0:1, 0:1]), "vC")
                    W(v, "vC")
                    v.tensor_tensor(out=pmb[:, 0:512], in0=q4d[:], in1=zodl2[:],
                                    op=ALU.subtract)
                    I(v.tensor_copy(out=stat[0:1, 1:2], in_=stat[0:1, 1:2]), "vC")
                    W(v, "vC")
                    I(v.tensor_tensor(out=wd16[:], in0=pmb[:, 0:512],
                                      in1=srep[:], op=ALU.mult), "vC")

                @block.tensor
                def _(t):
                    W(t, "vC")
                    W(t, "qH")
                    if ic == 0:
                        W(t, "aC")
                    for wt in range(8):
                        ins = t.matmul(dn_ps(wt),
                                       slab1[:, par * 1024 + wt * 128:par * 1024 + wt * 128 + 128],
                                       wd16[:],
                                       start=(ic == 0), stop=(ic == NIC - 1),
                                       skip_group_check=True)
                    I(ins, "pM")

            for wt in range(8):
                @block.vector
                def _(v, wt=wt):
                    if wt == 0:
                        W(v, "pM")
                    W(v, "qS")
                    I(v.tensor_copy(out=d16b[:, 0:512], in_=dn_ps(wt)), "vC")

                @block.scalar
                def _(sc, wt=wt):
                    W(sc, "vC")
                    I(sc.dma_start(
                        out=d16H[wt * 128:wt * 128 + 128, dmg * 512:dmg * 512 + 512],
                        in_=d16b[:, 0:512]), "qS", 16)

            @block.scalar
            def _(sc):
                W(sc, "qS")
                I(sc.copy(out=stat[0:2, 3:4], in_=stat[0:2, 3:4]), "aC")

        if "d16" in dbg_t:
            @block.gpsimd
            def _(g):
                W(g, "qS")
                I(g.dma_start(out=dbg_t["d16"][:, :], in_=d16H[:, :]), "oS", 16)

        for tc in range(8 if PH("ln2") else 0):
            emit_ln(x1sp, d16H, y, tc, b_is_f16=True)

        @block.gpsimd
        def _(g):
            W(g, "qH")
            W(g, "qS")
            W(g, "oS")
            g.memset(fcell[0:1, 3:4], 1)

    return nc, dbg_t


_CACHE = {}


def _build_in_maps(inputs):
    x = np.ascontiguousarray(np.asarray(inputs["L0_attn_hidden_states"])[0])
    wq = np.asarray(inputs["L0_attn_wq"])
    wkv = np.asarray(inputs["L0_attn_wkv"])
    wo = np.ascontiguousarray(np.asarray(inputs["L0_attn_wo"]))
    xT = np.ascontiguousarray(x.T)
    masks = _masks_np()
    bigg = _bigsel_g_np()
    bigd = _bigsel_d_np()
    gk = dict(
        g_q=np.ascontiguousarray(inputs["L0_ffn_gate_qweight"]),
        g_s=np.ascontiguousarray(inputs["L0_ffn_gate_scales"]),
        g_z=np.ascontiguousarray(inputs["L0_ffn_gate_zeros"]),
        u_q=np.ascontiguousarray(inputs["L0_ffn_up_qweight"]),
        u_s=np.ascontiguousarray(inputs["L0_ffn_up_scales"]),
        u_z=np.ascontiguousarray(inputs["L0_ffn_up_zeros"]),
        d_q=np.ascontiguousarray(inputs["L0_ffn_down_qweight"]),
        d_s=np.ascontiguousarray(inputs["L0_ffn_down_scales"]),
        d_z=np.ascontiguousarray(inputs["L0_ffn_down_zeros"]),
    )
    in_maps = []
    for c in range(8):
        p, h = c // 2, c % 2
        heads = list(range(16 * h, 16 * h + 16))
        own = p if h == 0 else 7 - p
        pair_cols = np.r_[CHTOK * p:CHTOK * p + CHTOK,
                          CHTOK * (7 - p):CHTOK * (7 - p) + CHTOK]
        qcols = np.concatenate([np.arange(hh * DQ, hh * DQ + DN) for hh in heads])
        kcols = np.concatenate([np.arange(hh * DN, hh * DN + DN) for hh in heads])
        vcols = np.concatenate([np.arange(H * DN + hh * DV, H * DN + hh * DV + DV)
                                for hh in heads])
        in_maps.append(dict(
            xT=xT,
            xqT=np.ascontiguousarray(xT[:, pair_cols]),
            xres=np.ascontiguousarray(x[CHTOK * own:CHTOK * own + CHTOK]),
            wq_s=np.ascontiguousarray(wq[:, qcols]),
            wk_s=np.ascontiguousarray(wkv[:, kcols]),
            wv_s=np.ascontiguousarray(wkv[:, vcols]),
            wo_f=wo,
            masksd=masks, bigg=bigg, bigd=bigd,
            moffd=_moff_np(p),
            cfg=np.array([[h, h * 2048, h * 1024, 0, 0, 0, 0, 0]], dtype=np.int32),
            **gk,
        ))
    return in_maps


def kernel(**inputs):
    in_maps = _build_in_maps(inputs)
    if "prog" not in _CACHE:
        nc, _ = build_program()
        nc.compile()
        _CACHE["prog"] = nc
    nc = _CACHE["prog"]
    res = run_bass_kernel_spmd(nc, in_maps, core_ids=list(range(8)))
    out = np.zeros((S, D), dtype=np.float32)
    for c in range(8):
        p, h = c // 2, c % 2
        own = p if h == 0 else 7 - p
        out[CHTOK * own:CHTOK * own + CHTOK] = np.asarray(res.results[c]["y"])
    return out.reshape(1, S, D)


# version 06 (final)
# speedup vs baseline: 1.1008x; 1.1008x over previous
import contextlib
import numpy as np
import ml_dtypes
import concourse.bass as bass
import concourse.mybir as mybir
from concourse.bacc import Bacc
from concourse.bass_utils import run_bass_kernel_spmd

F32 = mybir.dt.float32
F16 = mybir.dt.float16
BF16 = mybir.dt.bfloat16
I32 = mybir.dt.int32
I16 = mybir.dt.int16
AF = mybir.ActivationFunctionType
ALU = mybir.AluOpType
AX = mybir.AxisListType
ds = bass.ds

S = 8192
D = 4096
H = 32
DN = 128
DV = 128
DQ = 192
INTER = 11008
NH = 16
HG = 4
CHTOK = 1024
SCALE = float(DQ) ** -0.5
EPS = 1e-5
MAGIC0 = 0x5A5A1234
MAGIC1 = 0x0BADF00D

NKT = S // 128
NSW = 4
NIW = 22
IW_COLS = [512] * 21 + [256]
NIC = INTER // 128
NFC = D // 128


def _masks_np():
    kj = np.arange(128)[:, None]
    qj = np.arange(512)[None, :]
    parts = [(qj >= 128 * mi + kj).astype(np.float32) for mi in range(4)]
    parts.append(np.ones((128, 512), np.float32))
    parts.append(np.zeros((128, 512), np.float32))
    return np.concatenate(parts, axis=1).astype(ml_dtypes.bfloat16)


def _moff_np(p):
    out = np.zeros((NSW, NKT), dtype=np.int32)
    for sw in range(NSW):
        ch = p if sw < 2 else 7 - p
        qbase = CHTOK * ch + 512 * (sw % 2)
        for kt in range(NKT):
            ks = kt * 128
            if ks + 128 <= qbase:
                idx = 4
            elif ks >= qbase + 512:
                idx = 5
            else:
                idx = (ks - qbase) // 128
            out[sw, kt] = idx * 512
    return np.ascontiguousarray(out.reshape(1, NSW * NKT))


def _bigsel_g_np():
    b = np.zeros((128, 56), dtype=np.float32)
    for j in range(128):
        b[j, 24 + j // 16] = 1.0
    return b.astype(ml_dtypes.bfloat16)


def _bigsel_d_np():
    b = np.zeros((128, 172), dtype=np.float32)
    b[:, 86] = 1.0
    return b.astype(ml_dtypes.bfloat16)


class Ctr:
    def __init__(self):
        self.v = {}

    def inc(self, name, n):
        self.v[name] = self.v.get(name, 0) + n
        return self.v[name]

    def val(self, name):
        return self.v.get(name, 0)


PH_ORDER = ["consts", "zp", "kv", "attn", "bar", "wo", "ln1", "ffnt", "gateup", "down", "ln2"]


def build_program(dbg=(), stop_after=None):
    lim = len(PH_ORDER) if stop_after is None else PH_ORDER.index(stop_after)

    def PH(name):
        return PH_ORDER.index(name) <= lim

    nc = Bacc("TRN2", target_bir_lowering=False)

    def din(name, shape, dt):
        return nc.dram_tensor(name, shape, dt, kind="ExternalInput").ap()

    xT = din("xT", [D, S], F32)
    xqT = din("xqT", [D, 2 * CHTOK], F32)
    xres = din("xres", [CHTOK, D], F32)
    wq_s = din("wq_s", [D, NH * DN], F32)
    wk_s = din("wk_s", [D, NH * DN], F32)
    wv_s = din("wv_s", [D, NH * DV], F32)
    wo_f = din("wo_f", [H * DV, D], F32)
    g_q = din("g_q", [D // 8, INTER], I32)
    g_s = din("g_s", [32, INTER], F16)
    g_z = din("g_z", [32, INTER // 8], I32)
    u_q = din("u_q", [D // 8, INTER], I32)
    u_s = din("u_s", [32, INTER], F16)
    u_z = din("u_z", [32, INTER // 8], I32)
    d_q = din("d_q", [INTER // 8, D], I32)
    d_s = din("d_s", [NIC, D], F16)
    d_z = din("d_z", [NIC, D // 8], I32)
    masksd = din("masksd", [128, 6 * 512], BF16)
    moffd = din("moffd", [1, NSW * NKT], I32)
    bigg = din("bigg", [128, 56], BF16)
    bigd = din("bigd", [128, 172], BF16)
    cfg = din("cfg", [1, 8], I32)

    y = nc.dram_tensor("y", [CHTOK, D], F32, kind="ExternalOutput").ap()
    tmg = nc.dram_tensor("tmg", [1, 4], I32, kind="ExternalOutput").ap()
    dbg_t = {}
    if dbg:
        shp = {
            "kv": ([128, S + 32 * 512], BF16),
            "q": ([128, 2 * CHTOK], BF16),
            "attn": ([H * DV, 2 * CHTOK], BF16),
            "wo": ([CHTOK, D], F32),
            "x1": ([CHTOK, D], F32),
            "inter": ([256, CHTOK], BF16),
            "d16": ([CHTOK, D], F16),
        }
        for k in dbg:
            sh, dt = shp[k]
            dbg_t[k] = nc.dram_tensor("dbg_" + k, sh, dt, kind="ExternalOutput").ap()

    kT16 = nc.dram_tensor("kT16", [NH * 128, S], BF16).ap()
    V16 = nc.dram_tensor("V16", [HG * 128, (S // 128) * 512], BF16).ap()
    qT16 = nc.dram_tensor("qT16", [NH * 128, 2 * CHTOK], BF16).ap()
    wo_out = nc.dram_tensor("wo_out", [CHTOK, D], F32).ap()
    x1sp = nc.dram_tensor("x1sp", [CHTOK, D], F32).ap()
    interH = nc.dram_tensor("interH", [NIC * 128, CHTOK], BF16).ap()
    d16H = nc.dram_tensor("d16H", [CHTOK, D], F16).ap()
    zpgH = nc.dram_tensor("zpgH", [32, INTER], BF16).ap()
    zpuH = nc.dram_tensor("zpuH", [32, INTER], BF16).ap()
    zpdH = nc.dram_tensor("zpdH", [NIC, D], BF16).ap()

    attnT_sh = nc.dram_tensor("attnT_sh", [H * DV, 2 * CHTOK], BF16, addr_space="Shared").ap()
    flags = nc.dram_tensor("flags", [2, 16], I32, addr_space="Shared").ap()

    C = Ctr()

    with contextlib.ExitStack() as stk:
        def sb(name, shape, dt):
            return stk.enter_context(nc.sbuf_tensor(name, shape, dt))

        masks = sb("masks", [128, 6 * 512], BF16)
        moffs = sb("moffs", [1, NSW * NKT], I32)
        cfgs = sb("cfgs", [1, 8], I32)
        ones128 = sb("ones128", [128, 128], BF16)
        biggs = sb("biggs", [128, 56], BF16)
        bigds = sb("bigds", [128, 172], BF16)
        shv = sb("shv", [128, 1], I32)
        sh16 = [sb(f"sh16_{r}", [128, 1], I16) for r in range(4)]
        mk16 = sb("mk16", [128, 1], I16)
        epsv = sb("epsv", [128, 1], F32)
        fcell = sb("fcell", [1, 4], I32)
        fpoll = sb("fpoll", [1, 4], I32)

        slab0 = sb("slab0", [128, 16384], BF16)
        slab1 = sb("slab1", [128, 16384], BF16)
        lnA = slab0[:].bitcast(F32)[:, 0:4096]
        lnB = slab0[:].bitcast(F32)[:, 4096:8192]
        dump = slab1[:].bitcast(F32)[:, 0:4096]
        x1w16 = slab1[:, 8192:12288]
        arena = sb("arena", [128, 49152], BF16)
        evst = sb("evst", [128, 4 * 512], BF16)
        evf = sb("evf", [128, 1024], F32)
        pbuf = sb("pbuf", [128, 2 * 512], BF16)
        pmb = sb("pmb", [128, 2 * 512], BF16)
        recb = sb("recb", [128, 512], F32)
        stat = sb("stat", [128, 8], F32)

        s2b = sb("s2b", [128, 4 * 1024], BF16)
        zob = sb("zob", [128, 4 * 1024], BF16)
        zodl2 = sb("zodl2", [128, 512], BF16)
        silu_sb = sb("silu_sb", [128, 4 * CHTOK], BF16)
        ichunk = slab1[:, 14848:15872]
        srep = slab1[:, 13824:14336]
        wd16 = slab1[:, 13312:13824]
        q4d = slab1[:, 12288:13312].bitcast(I32)
        q4g = slab1[:, 12288:13312].bitcast(I16)
        d16b = slab1[:, 14336:14848].bitcast(F16)

        ps1 = stk.enter_context(nc.psum_tensor("ps1", [128, 2048], F32))
        ps2 = stk.enter_context(nc.psum_tensor("ps2", [128, 2048], F32))

        sem_names = ["qL", "qH", "qS", "vC", "aC", "pM", "oS", "bR", "bI"]
        sems = {n: stk.enter_context(nc.semaphore(n)) for n in sem_names}
        block = stk.enter_context(nc.Block())
        regs = {}

        def W(eng, name, val=None):
            eng.wait_ge(sems[name], C.val(name) if val is None else val)

        def I(ins, name, n=1):
            ins.then_inc(sems[name], n)
            return C.inc(name, n)

        @block.sync
        def _(sync):
            I(sync.dma_start(out=masks[:], in_=masksd[:, :]), "qH", 16)
            I(sync.dma_start(out=moffs[:], in_=moffd[:, :]), "qH", 16)
            I(sync.dma_start(out=cfgs[:], in_=cfg[:, :]), "qH", 16)
            I(sync.dma_start(out=biggs[:], in_=bigg[:, :]), "qH", 16)
            I(sync.dma_start(out=bigds[:], in_=bigd[:, :]), "qH", 16)

        @block.gpsimd
        def _(g):
            g.iota(shv[:], pattern=[[0, 1]], base=0, channel_multiplier=4)
            I(g.memset(fcell[:], 0), "bR")

        @block.vector
        def _(v):
            v.memset(ones128[:], 1.0)
            for r in range(4):
                v.memset(sh16[r][:], 4 * r)
            v.memset(mk16[:], 15)
            v.memset(epsv[:], EPS)
            W(v, "bR")
            I(v.tensor_scalar(out=shv[:], in0=shv[:], scalar1=31, scalar2=None,
                              op0=ALU.bitwise_and), "vC")

        def emit_zp(src_z, dstH, rows, zcols):
            CC = 2752 if zcols == INTER else 1024
            for cc in range(zcols // CC):
                @block.sync
                def _(sync):
                    W(sync, "qS")
                    W(sync, "vC")
                    I(sync.dma_start(out=lnA[:].bitcast(I32)[0:rows, 0:CC // 8],
                                     in_=src_z[:, cc * CC // 8:(cc + 1) * CC // 8]), "qH", 16)

                @block.vector
                def _(v):
                    W(v, "qH")
                    zi = dump[:].bitcast(I32)
                    for r in range(8):
                        v.tensor_scalar(
                            out=zi[0:rows, r:CC:8],
                            in0=lnA[:].bitcast(I32)[0:rows, 0:CC // 8],
                            scalar1=4 * r, scalar2=15,
                            op0=ALU.logical_shift_right, op1=ALU.bitwise_and)
                    I(v.tensor_scalar(out=shv[0:1, 0:1], in0=shv[0:1, 0:1], scalar1=0,
                                      scalar2=None, op0=ALU.bitwise_or), "vC")
                    W(v, "vC")
                    I(v.tensor_scalar(
                        out=slab1[0:rows, 8192:8192 + CC],
                        in0=zi[0:rows, 0:CC], scalar1=1.0, scalar2=None,
                        op0=ALU.add), "vC")

                @block.scalar
                def _(sc):
                    W(sc, "vC")
                    I(sc.dma_start(out=dstH[0:rows, cc * CC:(cc + 1) * CC],
                                   in_=slab1[0:rows, 8192:8192 + CC]), "qS", 16)

        if PH("zp"):
            emit_zp(g_z, zpgH, 32, INTER)
            emit_zp(u_z, zpuH, 32, INTER)
            emit_zp(d_z, zpdH, NIC, D)

        wq16 = arena[:, 0:16384]
        wk16 = arena[:, 16384:32768]
        wv16 = arena[:, 32768:49152]
        slabs = [slab0, slab1]
        vC_hist = {}

        for hg in range(HG if PH("kv") else 0):
            @block.gpsimd
            def _(g):
                W(g, "qS")
                W(g, "pM")
                for fc in range(NFC):
                    I(g.dma_start(out=wq16[:, fc * 512:fc * 512 + 512],
                                  in_=wq_s[fc * 128:fc * 128 + 128, hg * 512:hg * 512 + 512]), "qL", 16)
                    I(g.dma_start(out=wk16[:, fc * 512:fc * 512 + 512],
                                  in_=wk_s[fc * 128:fc * 128 + 128, hg * 512:hg * 512 + 512]), "qL", 16)
                    I(g.dma_start(out=wv16[:, fc * 512:fc * 512 + 512],
                                  in_=wv_s[fc * 128:fc * 128 + 128, hg * 512:hg * 512 + 512]), "qL", 16)
            w16_qL = C.val("qL")

            def kv_window(win, wsrc, c0, is_q, hg=hg, w16_qL=w16_qL):
                par = win % 2
                sl = slabs[par]

                @block.gpsimd
                def _(g):
                    if win >= 2:
                        W(g, "vC", vC_hist[(hg, win - 2)])
                    for fc in range(NFC):
                        I(g.dma_start(out=sl[:, fc * 512:fc * 512 + 512],
                                      in_=wsrc[fc * 128:fc * 128 + 128, c0:c0 + 512]), "qL", 16)

                @block.tensor
                def _(t):
                    W(t, "qL")
                    if win == 0:
                        W(t, "qL", w16_qL)
                    if win >= 1:
                        W(t, "vC", vC_hist[(hg, win - 1)])
                    for fc in range(NFC):
                        st = fc == 0
                        sp = fc == NFC - 1
                        wsel = wq16 if is_q else wk16
                        for hc in range(4):
                            ins = t.matmul(ps1[:, hc * 512:hc * 512 + 512],
                                           wsel[:, fc * 512 + hc * 128:fc * 512 + hc * 128 + 128],
                                           sl[:, fc * 512:fc * 512 + 512],
                                           start=st, stop=sp, skip_group_check=True)
                            if sp and hc == 3:
                                I(ins, "pM")
                        if not is_q:
                            for tc in range(4):
                                ins = t.matmul(ps2[:, tc * 512:tc * 512 + 512],
                                               sl[:, fc * 512 + tc * 128:fc * 512 + tc * 128 + 128],
                                               wv16[:, fc * 512:fc * 512 + 512],
                                               start=st, stop=sp, skip_group_check=True)
                                if sp and tc == 3:
                                    I(ins, "pM")

                @block.vector
                def _(v):
                    W(v, "pM")
                    W(v, "qH")
                    for j in range(4):
                        v.tensor_copy(out=evst[:, j * 512:j * 512 + 512],
                                      in_=ps1[:, j * 512:j * 512 + 512])
                    if not is_q:
                        for j in range(4):
                            v.tensor_copy(out=evf[:].bitcast(BF16)[:, j * 512:j * 512 + 512],
                                          in_=ps2[:, j * 512:j * 512 + 512])
                    I(v.tensor_copy(out=recb[0:1, 0:1], in_=recb[0:1, 0:1]), "vC")
                vC_hist[(hg, win)] = C.val("vC")

                @block.sync
                def _(sync):
                    W(sync, "vC")
                    if not is_q:
                        for hc in range(4):
                            I(sync.dma_start(
                                out=kT16[(hg * 4 + hc) * 128:(hg * 4 + hc) * 128 + 128,
                                         win * 512:win * 512 + 512],
                                in_=evst[:, hc * 512:hc * 512 + 512]), "qH", 16)
                        for tc in range(4):
                            I(sync.dma_start(
                                out=V16[hg * 128:hg * 128 + 128,
                                        (4 * win + tc) * 512:(4 * win + tc) * 512 + 512],
                                in_=evf[:].bitcast(BF16)[:, tc * 512:tc * 512 + 512]), "qH", 16)
                    else:
                        for hc in range(4):
                            I(sync.dma_start(
                                out=qT16[(hg * 4 + hc) * 128:(hg * 4 + hc) * 128 + 128,
                                         (win - 16) * 512:(win - 16) * 512 + 512],
                                in_=evst[:, hc * 512:hc * 512 + 512]), "qH", 16)

            for win in range(16):
                kv_window(win, xT, win * 512, False)
            for win in range(16, 20):
                kv_window(win, xqT, (win - 16) * 512, True)

        if "kv" in dbg_t:
            @block.gpsimd
            def _(g):
                W(g, "qH")
                I(g.dma_start(out=dbg_t["kv"][:, 0:S], in_=kT16[0:128, :]), "oS", 16)
                I(g.dma_start(out=dbg_t["kv"][:, S:S + 16384], in_=V16[0:128, 0:16384]), "oS", 16)

        if "q" in dbg_t:
            @block.gpsimd
            def _(g):
                W(g, "qH")
                I(g.dma_start(out=dbg_t["q"][:, :], in_=qT16[0:128, :]), "oS", 16)

        ktsb = arena[:, 0:32768]
        qtsb = arena[:, 32768:40960]
        vsb = [slab0, slab1]

        for hg in range(HG if PH("attn") else 0):
            @block.sync
            def _(sync):
                W(sync, "oS")
                W(sync, "qH")
                W(sync, "pM")
                for hi in range(4):
                    I(sync.dma_start(out=ktsb[:, hi * 8192:(hi + 1) * 8192],
                                     in_=kT16[(hg * 4 + hi) * 128:(hg * 4 + hi) * 128 + 128, :]), "qH", 16)
                    I(sync.dma_start(out=qtsb[:, hi * 2048:(hi + 1) * 2048],
                                     in_=qT16[(hg * 4 + hi) * 128:(hg * 4 + hi) * 128 + 128, :]), "qH", 16)
                I(sync.dma_start(out=slab0[:], in_=V16[hg * 128:hg * 128 + 128, 0:16384]), "qH", 16)
                I(sync.dma_start(out=slab1[:], in_=V16[hg * 128:hg * 128 + 128, 16384:32768]), "qH", 16)
            ld_qH = C.val("qH")

            for hi in range(4):
                for sw in range(NSW):
                    sVal = {}
                    expVal = {}
                    maskVal = {}
                    pvVal = {}

                    def emit_pv(t, kt, hi=hi):
                        t.wait_ge(sems["vC"], maskVal[kt])
                        t.matmul(ps2[:, 0:512],
                                 vsb[kt // 32][:, (kt % 32) * 512 + hi * 128:
                                               (kt % 32) * 512 + hi * 128 + 128],
                                 pmb[:, (kt % 2) * 512:(kt % 2) * 512 + 512],
                                 start=(kt == 0), stop=(kt == NKT - 1),
                                 skip_group_check=True)
                        ins = t.matmul(ps2[:, 512:1024], ones128[:],
                                       pmb[:, (kt % 2) * 512:(kt % 2) * 512 + 512],
                                       start=(kt == 0), stop=(kt == NKT - 1),
                                       skip_group_check=True)
                        ins.then_inc(sems["pM"], 2)
                        C.inc("pM", 2)
                        pvVal[kt] = C.val("pM")

                    for kt in range(NKT):
                        slot = kt % 4
                        spsum = ps1[:, slot * 512:slot * 512 + 512]

                        @block.tensor
                        def _(t):
                            if kt == 0 and hi == 0 and sw == 0:
                                W(t, "qH", ld_qH)
                            if kt >= 4:
                                t.wait_ge(sems["aC"], expVal[kt - 4])
                            ins = t.matmul(
                                spsum,
                                ktsb[:, hi * 8192 + kt * 128:hi * 8192 + kt * 128 + 128],
                                qtsb[:, hi * 2048 + sw * 512:hi * 2048 + sw * 512 + 512],
                                start=True, stop=True, skip_group_check=True)
                            sVal[kt] = I(ins, "pM")
                            if kt >= 2:
                                emit_pv(t, kt - 2)

                        @block.scalar
                        def _(sc):
                            sc.wait_ge(sems["pM"], sVal[kt])
                            if kt >= 2:
                                sc.wait_ge(sems["vC"], maskVal[kt - 2])
                            ins = sc.activation(out=pbuf[:, (kt % 2) * 512:(kt % 2) * 512 + 512],
                                                in_=spsum, func=AF.Exp, scale=SCALE)
                            expVal[kt] = I(ins, "aC")

                        @block.vector
                        def _(v):
                            v.wait_ge(sems["aC"], expVal[kt])
                            if kt >= 2:
                                v.wait_ge(sems["pM"], pvVal[kt - 2])
                            if "mr" not in regs:
                                regs["mr"] = v.alloc_register("mr")
                            v.load([regs["mr"]], moffs[0:1, sw * NKT + kt:sw * NKT + kt + 1])
                            mo = bass.RuntimeValue(regs["mr"], min_val=0, max_val=5 * 512)
                            ins = v.tensor_tensor(out=pmb[:, (kt % 2) * 512:(kt % 2) * 512 + 512],
                                                  in0=pbuf[:, (kt % 2) * 512:(kt % 2) * 512 + 512],
                                                  in1=masks[:, ds(mo, 512)], op=ALU.mult)
                            maskVal[kt] = I(ins, "vC")

                    @block.tensor
                    def _(t):
                        emit_pv(t, NKT - 2)
                        emit_pv(t, NKT - 1)

                    @block.vector
                    def _(v):
                        W(v, "pM")
                        W(v, "oS")
                        I(v.reciprocal(out=recb[:], in_=ps2[:, 512:1024]), "vC")
                        W(v, "vC")
                        I(v.tensor_tensor(out=evst[:, 0:512], in0=ps2[:, 0:512],
                                          in1=recb[:], op=ALU.mult), "vC")

                    @block.gpsimd
                    def _(g):
                        W(g, "vC")
                        if "rr" not in regs:
                            regs["rr"] = g.alloc_register("rowbase")
                            g.load([regs["rr"]], cfgs[0:1, 1:2])
                        rv = bass.RuntimeValue(regs["rr"], min_val=0, max_val=2048)
                        I(g.dma_start(
                            out=attnT_sh[ds(rv + (hg * 4 + hi) * 128, 128),
                                         sw * 512:sw * 512 + 512],
                            in_=evst[:, 0:512]), "oS", 16)

                    @block.tensor
                    def _(t):
                        W(t, "vC")

        @block.gpsimd
        def _(g):
            if not PH("bar"):
                return
            W(g, "oS")
            g.memset(fcell[0:1, 0:1], MAGIC0)
            g.memset(fcell[0:1, 1:2], MAGIC1)
            I(g.memset(fcell[0:1, 2:3], 0), "bR")
            W(g, "bR")
            regs["hr"] = g.alloc_register("hreg")
            g.load([regs["hr"]], cfgs[0:1, 0:1])
            hv = bass.RuntimeValue(regs["hr"], min_val=0, max_val=1)
            I(g.dma_start(out=flags[ds(hv, 1), 0:4], in_=fcell[:, :]), "oS", 16)
            W(g, "oS")
            cnt = g.alloc_register("cnt")
            g.reg_mov(cnt, C.val("bR"))
            r0 = g.alloc_register("m0")
            r1 = g.alloc_register("m1")
            okr = g.alloc_register("ok")
            g.reg_mov(okr, 0)

            def cond():
                return bass.RuntimeValue(okr, min_val=0, max_val=1) == 0

            with g.While(cond):
                g.reg_add(cnt, cnt, 16)
                g.dma_start(out=fpoll[:, :], in_=flags[ds(1 - hv, 1), 0:4]).then_inc(sems["bR"], 16)
                g.wait_ge(sems["bR"], bass.RuntimeValue(cnt, min_val=0))
                g.load([r0, r1], fpoll[0:1, 0:2])
                a = bass.RuntimeValue(r0) == MAGIC0
                b = bass.RuntimeValue(r1) == MAGIC1
                g.reg_alu(okr, a, b, op=ALU.logical_and)
            g.reg_save(fcell[0:1, 3:4], bass.RuntimeValue(cnt, min_val=0))
            I(g.dma_start(out=tmg[:, :], in_=fcell[:, :]), "oS", 16)
            regs["cr"] = g.alloc_register("creg")
            g.load([regs["cr"]], cfgs[0:1, 2:3])
            cv = bass.RuntimeValue(regs["cr"], min_val=0, max_val=1024)
            for dc in range(32):
                I(g.dma_start(out=arena[:, dc * 1024:dc * 1024 + 1024],
                              in_=attnT_sh[dc * 128:dc * 128 + 128, ds(cv, 1024)]), "qL", 16)

        if "attn" in dbg_t:
            @block.gpsimd
            def _(g):
                I(g.dma_start(out=dbg_t["attn"][:, :], in_=attnT_sh[:, :]), "oS", 16)

        attw = arena
        wo_buf = [slab0, slab1]
        for dw in range(8 if PH("wo") else 0):
            par = dw % 2

            @block.gpsimd
            def _(g):
                if dw >= 2:
                    W(g, "pM")
                for dc in range(32):
                    I(g.dma_start(out=wo_buf[par][:, dc * 512:dc * 512 + 512],
                                  in_=wo_f[dc * 128:dc * 128 + 128, dw * 512:dw * 512 + 512]), "qL", 16)

            for tc in range(8):
                ppar = tc % 2

                @block.tensor
                def _(t):
                    W(t, "qL")
                    W(t, "vC")
                    for dc in range(32):
                        ins = t.matmul(ps1[:, ppar * 512:ppar * 512 + 512],
                                       attw[:, dc * 1024 + tc * 128:dc * 1024 + tc * 128 + 128],
                                       wo_buf[par][:, dc * 512:dc * 512 + 512],
                                       start=(dc == 0), stop=(dc == 31), skip_group_check=True)
                    I(ins, "pM")

                @block.vector
                def _(v):
                    W(v, "pM")
                    W(v, "qS")
                    I(v.tensor_copy(out=evf[:, ppar * 512:ppar * 512 + 512],
                                    in_=ps1[:, ppar * 512:ppar * 512 + 512]), "vC")

                @block.scalar
                def _(sc):
                    W(sc, "vC")
                    I(sc.dma_start(out=wo_out[tc * 128:tc * 128 + 128, dw * 512:dw * 512 + 512],
                                   in_=evf[:, ppar * 512:ppar * 512 + 512]), "qS", 16)

        if "wo" in dbg_t:
            @block.gpsimd
            def _(g):
                W(g, "qS")
                I(g.dma_start(out=dbg_t["wo"][:, :], in_=wo_out[:, :]), "oS", 16)

        def emit_ln(srcA, srcB, dstH, tc, b_is_f16=False):
            @block.sync
            def _(sync):
                W(sync, "qH")
                W(sync, "aC")
                W(sync, "qS")
                W(sync, "qL")
                W(sync, "pM")
                I(sync.dma_start(out=lnA[:], in_=srcA[tc * 128:tc * 128 + 128, :]), "qH", 16)
                if b_is_f16:
                    I(sync.dma_start(out=lnB[:].bitcast(F16)[:, 0:4096],
                                     in_=srcB[tc * 128:tc * 128 + 128, :]), "qH", 16)
                else:
                    I(sync.dma_start(out=lnB[:], in_=srcB[tc * 128:tc * 128 + 128, :]), "qH", 16)

            @block.vector
            def _(v):
                W(v, "qH")
                bop = lnB[:].bitcast(F16)[:, 0:4096] if b_is_f16 else lnB[:]
                I(v.tensor_tensor(out=lnA[:], in0=lnA[:], in1=bop, op=ALU.add), "vC")
                W(v, "vC")
                I(v.reduce_sum(stat[:, 0:1], lnA[:], axis=AX.X), "vC")

            @block.scalar
            def _(sc):
                W(sc, "vC")
                I(sc.activation(out=dump[:], in_=lnA[:], func=AF.Square,
                                accum_out=stat[:, 1:2]), "aC")

            @block.vector
            def _(v):
                W(v, "aC")
                v.tensor_scalar(out=stat[:, 2:3], in0=stat[:, 0:1], scalar1=1.0 / D,
                                scalar2=None, op0=ALU.mult)
                I(v.tensor_scalar(out=stat[:, 3:4], in0=stat[:, 1:2], scalar1=1.0 / D,
                                  scalar2=None, op0=ALU.mult), "vC")
                W(v, "vC")
                I(v.tensor_tensor(out=stat[:, 4:5], in0=stat[:, 2:3], in1=stat[:, 2:3],
                                  op=ALU.mult), "vC")
                W(v, "vC")
                I(v.tensor_tensor(out=stat[:, 5:6], in0=stat[:, 3:4], in1=stat[:, 4:5],
                                  op=ALU.subtract), "vC")

            @block.scalar
            def _(sc):
                W(sc, "vC")
                I(sc.activation(out=stat[:, 6:7], in_=stat[:, 5:6], func=AF.Sqrt,
                                bias=epsv[:]), "aC")

            @block.vector
            def _(v):
                W(v, "aC")
                I(v.reciprocal(out=stat[:, 0:1], in_=stat[:, 6:7]), "vC")
                W(v, "vC")
                I(v.tensor_tensor(out=stat[:, 7:8], in0=stat[:, 2:3], in1=stat[:, 0:1],
                                  op=ALU.mult), "vC")
                W(v, "vC")
                I(v.tensor_scalar(out=stat[:, 7:8], in0=stat[:, 7:8], scalar1=-1.0,
                                  scalar2=None, op0=ALU.mult), "vC")

            @block.scalar
            def _(sc):
                W(sc, "vC")
                I(sc.activation(out=dump[:], in_=lnA[:], func=AF.Identity,
                                scale=stat[:, 0:1], bias=stat[:, 7:8]), "aC")

            @block.sync
            def _(sync):
                W(sync, "aC")
                I(sync.dma_start(out=dstH[tc * 128:tc * 128 + 128, :], in_=dump[:]), "qH", 16)

        for tc in range(8 if PH("ln1") else 0):
            emit_ln(wo_out, xres, x1sp, tc)

        if "x1" in dbg_t:
            @block.gpsimd
            def _(g):
                W(g, "qH")
                I(g.dma_start(out=dbg_t["x1"][:, :], in_=x1sp[:, :]), "oS", 16)

        x1T = arena
        idsb = recb[:, 0:128]

        @block.gpsimd
        def _(g):
            if not PH("ffnt"):
                return
            W(g, "qH")
            W(g, "pM")
            g.memset(idsb[:], 0.0)
            I(g.affine_select(out=idsb[:], in_=idsb[:], compare_op=ALU.not_equal,
                              fill=1.0, base=0, pattern=[[-1, 128]],
                              channel_multiplier=1), "bI")
        idsb_bR = C.val("bI")

        for tc in range(8 if PH("ffnt") else 0):
            @block.sync
            def _(sync):
                W(sync, "pM")
                if tc == 0:
                    W(sync, "qH")
                    W(sync, "aC")
                I(sync.dma_start(out=lnA[:], in_=x1sp[tc * 128:tc * 128 + 128, :]), "qH", 16)

            for ci in range(32):
                rho, jc = ci // 4, ci % 4

                @block.tensor
                def _(t):
                    if ci == 0:
                        W(t, "qH")
                        if tc == 0:
                            W(t, "bI", idsb_bR)
                            W(t, "vC")
                    if ci >= 2 or tc > 0:
                        W(t, "vC")
                    I(t.transpose(ps2[:, (ci % 2) * 512:(ci % 2) * 512 + 128],
                                  lnA[:, 1024 * jc + rho:1024 * jc + rho + 8 * 127 + 1:8],
                                  idsb[:]), "pM")

                @block.vector
                def _(v):
                    W(v, "pM")
                    I(v.tensor_copy(out=x1T[:, ci * 1024 + tc * 128:ci * 1024 + tc * 128 + 128],
                                    in_=ps2[:, (ci % 2) * 512:(ci % 2) * 512 + 128]), "vC")

        tileMM = {}
        evacGate = {"a": 0, "v": 0, "s": 0}
        unpackLast = {0: 0}
        storeQS = {0: 0, 1: 0}
        ichunkB = slab1[:, 13312:14336]

        def gup_ps(grp):
            off = grp * 256
            return (ps1[:, off:off + 256] if off < 2048
                    else ps2[:, off - 2048:off - 2048 + 256])

        def emit_mat(mat, iw):
            iwc = IW_COLS[iw]
            iw0 = iw * 512
            qsrc = g_q if mat == "g" else u_q
            ssrc = g_s if mat == "g" else u_s
            zH = zpgH if mat == "g" else zpuH
            is_gate = mat == "g"
            nnc = iwc // 128

            @block.sync
            def _(sync):
                if tileMM:
                    sync.wait_ge(sems["vC"], unpackLast[0])
                for jc in range(4):
                    sync.dma_start(
                        out=slab1[:, jc * 2048:jc * 2048 + 2 * iwc].bitcast(I32)[:, 0:iwc],
                        in_=qsrc[jc * 128:jc * 128 + 128, iw0:iw0 + iwc]).then_inc(sems["qH"], 16)
                    C.inc("qH", 16)
            qwVal = C.val("qH")

            @block.gpsimd
            def _(g):
                if tileMM:
                    g.wait_ge(sems["vC"], unpackLast[0])
                for jc in range(4):
                    src = ssrc[8 * jc:8 * jc + 8, iw0:iw0 + iwc]
                    src = src.unsqueeze(1).broadcast_to([8, 16, iwc])
                    g.dma_start(out=s2b[:, jc * 1024:jc * 1024 + iwc], in_=src).then_inc(sems["qL"], 16)
                    C.inc("qL", 16)
                    zsrc = zH[8 * jc:8 * jc + 8, iw0:iw0 + iwc]
                    zsrc = zsrc.unsqueeze(1).broadcast_to([8, 16, iwc])
                    g.dma_start(out=zob[:, jc * 1024:jc * 1024 + iwc], in_=zsrc).then_inc(sems["qL"], 16)
                    C.inc("qL", 16)
            s2Val = C.val("qL")

            unpackVal = {}
            for t in range(16):
                jc, rp = t // 4, t % 4

                @block.vector
                def _(v):
                    if t == 0:
                        v.wait_ge(sems["qH"], qwVal)
                        v.wait_ge(sems["qL"], s2Val)
                    if t in tileMM:
                        v.wait_ge(sems["pM"], tileMM[t])
                    qv16 = slab1[:, jc * 2048:jc * 2048 + 2 * iwc].bitcast(I16)
                    s2v = s2b[:, jc * 1024:jc * 1024 + iwc].unsqueeze(2)
                    s2v = s2v.broadcast_to([128, iwc, 2])
                    zov = zob[:, jc * 1024:jc * 1024 + iwc].unsqueeze(2)
                    zov = zov.broadcast_to([128, iwc, 2])
                    v.tensor_scalar(out=q4g[:, 0:2 * iwc], in0=qv16[:, 0:2 * iwc],
                                    scalar1=sh16[rp][:], scalar2=mk16[:],
                                    op0=ALU.logical_shift_right, op1=ALU.bitwise_and)
                    I(v.tensor_copy(out=stat[0:1, 0:1], in_=stat[0:1, 0:1]), "vC")
                    W(v, "vC")
                    v.tensor_tensor(out=pmb[:, 0:2 * iwc], in0=q4g[:, 0:2 * iwc],
                                    in1=zov, op=ALU.subtract)
                    I(v.tensor_copy(out=stat[0:1, 1:2], in_=stat[0:1, 1:2]), "vC")
                    W(v, "vC")
                    ins = v.tensor_tensor(
                        out=slab0[:, t * 1024:t * 1024 + 2 * iwc],
                        in0=pmb[:, 0:2 * iwc], in1=s2v, op=ALU.mult)
                    unpackVal[t] = I(ins, "vC")

                @block.tensor
                def _(tt):
                    tt.wait_ge(sems["vC"], unpackVal[t])
                    if t == 0:
                        tt.wait_ge(sems["aC"], evacGate["a"])
                        tt.wait_ge(sems["vC"], evacGate["v"])
                    ins = None
                    for hp in range(2):
                        for ncc in range(nnc):
                            for win in range(4):
                                grp = ncc * 4 + win
                                base = t * 1024 + 2 * (ncc * 128) + hp
                                wt = slab0[:, base:base + 2 * 127 + 1:2]
                                ci = (4 * hp + rp) * 4 + jc
                                ins = tt.matmul(
                                    gup_ps(grp), wt,
                                    x1T[:, ci * 1024 + win * 256:ci * 1024 + win * 256 + 256],
                                    start=(t == 0 and hp == 0 and win % 2 == 0),
                                    stop=(t == 15 and hp == 1 and win % 2 == 1),
                                    skip_group_check=True)
                    tileMM[t] = I(ins, "pM")
            unpackLast[0] = unpackVal[15]
            mmLast = tileMM[15]

            if is_gate:
                for grp in range(4 * nnc):
                    @block.scalar
                    def _(sc, grp=grp):
                        sc.wait_ge(sems["pM"], mmLast)
                        ncc, win = grp // 4, grp % 4
                        I(sc.activation(
                            out=silu_sb[:, ncc * 1024 + win * 256:ncc * 1024 + win * 256 + 256],
                            in_=gup_ps(grp), func=AF.Silu), "aC")
                evacGate["a"] = C.val("aC")
            else:
                for ncc in range(nnc):
                    ibuf = ichunk if ncc % 2 == 0 else ichunkB
                    for win in range(4):
                        grp = ncc * 4 + win

                        @block.vector
                        def _(v, grp=grp, ncc=ncc, win=win, ibuf=ibuf):
                            v.wait_ge(sems["pM"], mmLast)
                            if win == 0:
                                v.wait_ge(sems["qS"], storeQS[ncc % 2])
                            I(v.tensor_tensor(
                                out=ibuf[:, win * 256:win * 256 + 256],
                                in0=gup_ps(grp),
                                in1=silu_sb[:, ncc * 1024 + win * 256:ncc * 1024 + win * 256 + 256],
                                op=ALU.mult), "vC")

                    ic = iw * 4 + ncc

                    @block.scalar
                    def _(sc, ic=ic, ibuf=ibuf):
                        W(sc, "vC")
                        I(sc.dma_start(out=interH[ic * 128:ic * 128 + 128, :],
                                       in_=ibuf[:, 0:1024]), "qS", 16)
                        storeQS[ncc % 2] = C.val("qS")
                evacGate["v"] = C.val("vC")

        for iw in range(NIW if PH("gateup") else 0):
            emit_mat("g", iw)
            emit_mat("u", iw)

        if "inter" in dbg_t:
            @block.gpsimd
            def _(g):
                W(g, "qS")
                W(g, "vC")
                I(g.dma_start(out=dbg_t["inter"][:, :], in_=interH[0:256, :]), "oS", 16)

        def dn_ps(wt):
            return (ps1[:, (wt % 4) * 512:(wt % 4) * 512 + 512] if wt < 4
                    else ps2[:, (wt - 4) * 512:(wt - 4) * 512 + 512])

        for dmg in range(8 if PH("down") else 0):
            @block.sync
            def _(sync):
                W(sync, "aC")
                if dmg == 0:
                    W(sync, "qS")
                    W(sync, "vC")
                pass

            for ic in range(NIC):
                par = ic % 2

                @block.sync
                def _(sync):
                    W(sync, "pM")
                    I(sync.dma_start(out=slab1[:, par * 1024:par * 1024 + 1024],
                                     in_=interH[ic * 128:ic * 128 + 128, :]), "qH", 16)

                @block.gpsimd
                def _(g):
                    W(g, "pM")
                    if dmg == 0 and ic == 0:
                        W(g, "qS")
                    src = d_q[ic * 16:ic * 16 + 16, dmg * 512:dmg * 512 + 512]
                    src = src.unsqueeze(1).broadcast_to([16, 8, 512])
                    I(g.dma_start(out=q4d[:], in_=src), "qL", 16)
                    srow = d_s[ic:ic + 1, dmg * 512:dmg * 512 + 512].broadcast_to([128, 512])
                    I(g.dma_start(out=srep[:], in_=srow), "qL", 16)
                    zrow = zpdH[ic:ic + 1, dmg * 512:dmg * 512 + 512].broadcast_to([128, 512])
                    I(g.dma_start(out=zodl2[:], in_=zrow), "qL", 16)

                @block.vector
                def _(v):
                    W(v, "qL")
                    W(v, "pM")
                    v.tensor_scalar(out=q4d[:], in0=q4d[:],
                                    scalar1=shv[:], scalar2=15,
                                    op0=ALU.logical_shift_right, op1=ALU.bitwise_and)
                    I(v.tensor_copy(out=stat[0:1, 0:1], in_=stat[0:1, 0:1]), "vC")
                    W(v, "vC")
                    v.tensor_tensor(out=pmb[:, 0:512], in0=q4d[:], in1=zodl2[:],
                                    op=ALU.subtract)
                    I(v.tensor_copy(out=stat[0:1, 1:2], in_=stat[0:1, 1:2]), "vC")
                    W(v, "vC")
                    I(v.tensor_tensor(out=wd16[:], in0=pmb[:, 0:512],
                                      in1=srep[:], op=ALU.mult), "vC")

                @block.tensor
                def _(t):
                    W(t, "vC")
                    W(t, "qH")
                    if ic == 0:
                        W(t, "aC")
                    for wt in range(8):
                        ins = t.matmul(dn_ps(wt),
                                       slab1[:, par * 1024 + wt * 128:par * 1024 + wt * 128 + 128],
                                       wd16[:],
                                       start=(ic == 0), stop=(ic == NIC - 1),
                                       skip_group_check=True)
                    I(ins, "pM")

            for wt in range(8):
                @block.vector
                def _(v, wt=wt):
                    if wt == 0:
                        W(v, "pM")
                    W(v, "qS")
                    I(v.tensor_copy(out=d16b[:, 0:512], in_=dn_ps(wt)), "vC")

                @block.scalar
                def _(sc, wt=wt):
                    W(sc, "vC")
                    I(sc.dma_start(
                        out=d16H[wt * 128:wt * 128 + 128, dmg * 512:dmg * 512 + 512],
                        in_=d16b[:, 0:512]), "qS", 16)

            @block.scalar
            def _(sc):
                W(sc, "qS")
                I(sc.copy(out=stat[0:2, 3:4], in_=stat[0:2, 3:4]), "aC")

        if "d16" in dbg_t:
            @block.gpsimd
            def _(g):
                W(g, "qS")
                I(g.dma_start(out=dbg_t["d16"][:, :], in_=d16H[:, :]), "oS", 16)

        for tc in range(8 if PH("ln2") else 0):
            emit_ln(x1sp, d16H, y, tc, b_is_f16=True)

        @block.gpsimd
        def _(g):
            W(g, "qH")
            W(g, "qS")
            W(g, "oS")
            g.memset(fcell[0:1, 3:4], 1)

    return nc, dbg_t


_CACHE = {}


def _build_in_maps(inputs):
    x = np.ascontiguousarray(np.asarray(inputs["L0_attn_hidden_states"])[0])
    wq = np.asarray(inputs["L0_attn_wq"])
    wkv = np.asarray(inputs["L0_attn_wkv"])
    wo = np.ascontiguousarray(np.asarray(inputs["L0_attn_wo"]))
    xT = np.ascontiguousarray(x.T)
    masks = _masks_np()
    bigg = _bigsel_g_np()
    bigd = _bigsel_d_np()
    gk = dict(
        g_q=np.ascontiguousarray(inputs["L0_ffn_gate_qweight"]),
        g_s=np.ascontiguousarray(inputs["L0_ffn_gate_scales"]),
        g_z=np.ascontiguousarray(inputs["L0_ffn_gate_zeros"]),
        u_q=np.ascontiguousarray(inputs["L0_ffn_up_qweight"]),
        u_s=np.ascontiguousarray(inputs["L0_ffn_up_scales"]),
        u_z=np.ascontiguousarray(inputs["L0_ffn_up_zeros"]),
        d_q=np.ascontiguousarray(inputs["L0_ffn_down_qweight"]),
        d_s=np.ascontiguousarray(inputs["L0_ffn_down_scales"]),
        d_z=np.ascontiguousarray(inputs["L0_ffn_down_zeros"]),
    )
    in_maps = []
    for c in range(8):
        p, h = c // 2, c % 2
        heads = list(range(16 * h, 16 * h + 16))
        own = p if h == 0 else 7 - p
        pair_cols = np.r_[CHTOK * p:CHTOK * p + CHTOK,
                          CHTOK * (7 - p):CHTOK * (7 - p) + CHTOK]
        qcols = np.concatenate([np.arange(hh * DQ, hh * DQ + DN) for hh in heads])
        kcols = np.concatenate([np.arange(hh * DN, hh * DN + DN) for hh in heads])
        vcols = np.concatenate([np.arange(H * DN + hh * DV, H * DN + hh * DV + DV)
                                for hh in heads])
        in_maps.append(dict(
            xT=xT,
            xqT=np.ascontiguousarray(xT[:, pair_cols]),
            xres=np.ascontiguousarray(x[CHTOK * own:CHTOK * own + CHTOK]),
            wq_s=np.ascontiguousarray(wq[:, qcols]),
            wk_s=np.ascontiguousarray(wkv[:, kcols]),
            wv_s=np.ascontiguousarray(wkv[:, vcols]),
            wo_f=wo,
            masksd=masks, bigg=bigg, bigd=bigd,
            moffd=_moff_np(p),
            cfg=np.array([[h, h * 2048, h * 1024, 0, 0, 0, 0, 0]], dtype=np.int32),
            **gk,
        ))
    return in_maps


def kernel(**inputs):
    in_maps = _build_in_maps(inputs)
    if "prog" not in _CACHE:
        nc, _ = build_program()
        nc.compile()
        _CACHE["prog"] = nc
    nc = _CACHE["prog"]
    res = run_bass_kernel_spmd(nc, in_maps, core_ids=list(range(8)))
    out = np.zeros((S, D), dtype=np.float32)
    for c in range(8):
        p, h = c // 2, c % 2
        own = p if h == 0 else 7 - p
        out[CHTOK * own:CHTOK * own + CHTOK] = np.asarray(res.results[c]["y"])
    return out.reshape(1, S, D)
